# version 10
# speedup vs baseline: 1.8948x; 1.0058x over previous
import numpy as np
import ml_dtypes
from contextlib import ExitStack

import concourse.bass as bass
import concourse.mybir as mybir
from concourse.bass import IndirectOffsetOnAxis
from concourse.bass_utils import run_bass_kernel_spmd

F32 = mybir.dt.float32
BF16 = mybir.dt.bfloat16
I32 = mybir.dt.int32
AF = mybir.ActivationFunctionType
ALU = mybir.AluOpType
AX = mybir.AxisListType

T = 4096
D = 1024
DH = 64
RD = 32
KVR = 256
QRK = 512
E = 8
FF = 4096
OWN = 2048
NT = T // 128
NOT = OWN // 128
NB = T // 512
SCALE = 1.0 / float(np.sqrt(96.0))
NEG = -1e30
GL = 2560


class Sem:
    def __init__(self, ctx, nc, name):
        self.h = ctx.enter_context(nc.semaphore(name))
        self.n = 0

    def inc(self, instr, k=1):
        self.n += k
        return instr.then_inc(self.h, k)

    def incd(self, instr):
        return self.inc(instr, 16)


def _rope_tables():
    inv = 1.0 / 10000.0 ** (np.arange(0, RD, 2, dtype=np.float32) / RD)
    t = np.arange(T, dtype=np.float32)
    fr = np.outer(t, inv)
    emb = np.concatenate([fr, fr], -1)
    cos = np.ascontiguousarray(np.cos(emb).T)
    sin = np.ascontiguousarray(np.sin(emb).T)
    cos2 = np.concatenate([cos, cos], 0).astype(ml_dtypes.bfloat16)
    sin2 = np.concatenate([sin, sin], 0).astype(ml_dtypes.bfloat16)
    return np.ascontiguousarray(cos2), np.ascontiguousarray(sin2)


def _rotT():
    R = np.zeros((RD, RD), np.float32)
    for i in range(16):
        R[i, i + 16] = -1.0
        R[i + 16, i] = 1.0
    RT = R.T
    out = np.zeros((2 * RD, 2 * RD), np.float32)
    out[:RD, :RD] = RT
    out[RD:, RD:] = RT
    return np.ascontiguousarray(out.astype(ml_dtypes.bfloat16))


def _dz(dst2d, src2d, p=128):
    rows, cols = src2d.shape
    c = rows // p
    return (dst2d.rearrange("q (c r) -> q c r", c=c, r=cols),
            src2d.rearrange("(c q) r -> q c r", q=p))


def build_program():
    nc = bass.Bass("TRN2", target_bir_lowering=False,
                   dynamic_dma_scratch_size=24576)

    x_seq = nc.dram_tensor("x_seq", [T, D], F32, kind="ExternalInput")
    x_own = nc.dram_tensor("x_own", [OWN, D], F32, kind="ExternalInput")
    w_kv = nc.dram_tensor("w_kv", [D, KVR], F32, kind="ExternalInput")
    gkv_pc = nc.dram_tensor("gkv_pc", [128, 2], F32, kind="ExternalInput")
    w_kdec = nc.dram_tensor("w_kdec", [KVR, 512], F32, kind="ExternalInput")
    w_vdec = nc.dram_tensor("w_vdec", [KVR, 512], F32, kind="ExternalInput")
    w_q = nc.dram_tensor("w_q", [D, QRK], F32, kind="ExternalInput")
    w_qdec = nc.dram_tensor("w_qdec", [QRK, 512], F32, kind="ExternalInput")
    w_krope = nc.dram_tensor("w_krope", [D, 256], F32, kind="ExternalInput")
    w_qrope = nc.dram_tensor("w_qrope", [QRK, 256], F32, kind="ExternalInput")
    wo_a = nc.dram_tensor("wo_a", [512, D], F32, kind="ExternalInput")
    wo_b = nc.dram_tensor("wo_b", [512, D], F32, kind="ExternalInput")
    cent_pc = nc.dram_tensor("cent_pc", [128, 64], F32, kind="ExternalInput")
    lng_pc = nc.dram_tensor("lng_pc", [128, 64], F32, kind="ExternalInput")
    lnb_pc = nc.dram_tensor("lnb_pc", [128, 64], F32, kind="ExternalInput")
    b1_pc = nc.dram_tensor("b1_pc", [128, 256], F32, kind="ExternalInput")
    b2_pc = nc.dram_tensor("b2_pc", [128, 64], F32, kind="ExternalInput")
    w1t = nc.dram_tensor("w1t", [E, D, FF], F32, kind="ExternalInput")
    w2t = nc.dram_tensor("w2t", [E, FF, D], F32, kind="ExternalInput")
    ids_a_d = nc.dram_tensor("ids_a", [128, NOT], I32, kind="ExternalInput")
    ids_b_d = nc.dram_tensor("ids_b", [128, NOT], I32, kind="ExternalInput")
    out_own = nc.dram_tensor("out_own", [OWN, D], F32, kind="ExternalOutput")

    o_dram = nc.dram_tensor("o_dram", [T, 512], BF16, kind="Internal")
    og_dram = nc.dram_tensor("og_dram", [2 * T, 512], BF16, kind="Internal")
    s_dram = nc.dram_tensor("s_dram", [OWN, D], F32, kind="Internal")
    xhtm_dram = nc.dram_tensor("xhtm_dram", [OWN, D], BF16, kind="Internal")
    xg_dram = nc.dram_tensor("xg_dram", [E * GL, D], BF16, kind="Internal")
    krope_dram = nc.dram_tensor("krope_dram", [256, T], BF16, kind="Internal")
    qrope_dram = nc.dram_tensor("qrope_dram", [256, T], BF16, kind="Internal")
    dec_dram = nc.dram_tensor("dec_dram", [E * GL, D], BF16, kind="Internal")

    cos_np, sin_np = _rope_tables()
    cos_d = nc.inline_tensor(cos_np, name="cos_c")
    sin_d = nc.inline_tensor(sin_np, name="sin_c")
    rotT_d = nc.inline_tensor(_rotT(), name="rotT_c")
    identb_d = nc.inline_tensor(np.eye(128, dtype=ml_dtypes.bfloat16), name="identb_c")
    identf_d = nc.inline_tensor(np.eye(128, dtype=np.float32), name="identf_c")
    mneg = np.zeros((128, 128), np.float32)
    for q in range(128):
        mneg[q, q + 1:] = NEG
    maskneg_d = nc.inline_tensor(mneg, name="maskneg_c")
    ltri_np = np.zeros((128, 128), ml_dtypes.bfloat16)
    for tt in range(128):
        ltri_np[:tt, tt] = 1.0
    ltri_d = nc.inline_tensor(np.ascontiguousarray(ltri_np), name="ltri_c")
    basev_d = nc.inline_tensor(
        np.arange(E, dtype=np.float32).reshape(1, E) * float(GL), name="basev_c")

    ctx = ExitStack()
    sD = Sem(ctx, nc, "sD")
    sG = Sem(ctx, nc, "sG")
    sPE = Sem(ctx, nc, "sPE")
    sV = Sem(ctx, nc, "sV")
    sA = Sem(ctx, nc, "sA")
    sPL = Sem(ctx, nc, "sPL")
    sCC = Sem(ctx, nc, "sCC")

    V_ = nc.vector
    A_ = nc.scalar
    P_ = nc.tensor
    G_ = nc.gpsimd
    Y_ = nc.sync

    def ln_tile(xt, xnt, st, eps):
        V_.wait_ge(sD.h, sD.n)
        sV.inc(V_.reduce_sum(st[:, 0:1], xt[:, :], axis=AX.X))
        A_.wait_ge(sD.h, sD.n)
        sA.inc(A_.activation(lnsq[:, :], xt[:, :], AF.Square, accum_out=st[:, 1:2]))
        V_.wait_ge(sV.h, sV.n)
        sV.inc(V_.tensor_scalar_mul(st[:, 2:3], st[:, 0:1], 1.0 / D))
        V_.wait_ge(sA.h, sA.n)
        V_.wait_ge(sV.h, sV.n)
        sV.inc(V_.tensor_scalar_mul(st[:, 3:4], st[:, 1:2], 1.0 / D))
        V_.wait_ge(sV.h, sV.n)
        sV.inc(V_.tensor_mul(st[:, 4:5], st[:, 2:3], st[:, 2:3]))
        V_.wait_ge(sV.h, sV.n)
        sV.inc(V_.tensor_sub(st[:, 5:6], st[:, 3:4], st[:, 4:5]))
        V_.wait_ge(sV.h, sV.n)
        sV.inc(V_.tensor_scalar_add(st[:, 6:7], st[:, 5:6], eps))
        A_.wait_ge(sV.h, sV.n)
        sA.inc(A_.activation(st[:, 7:8], st[:, 6:7], AF.Ln))
        A_.wait_ge(sA.h, sA.n)
        sA.inc(A_.activation(st[:, 6:7], st[:, 7:8], AF.Exp, scale=-0.5))
        V_.wait_ge(sA.h, sA.n)
        sV.inc(V_.tensor_scalar(xnt[:, :], xt[:, :], st[:, 2:3], st[:, 6:7],
                                ALU.subtract, ALU.mult))

    with ctx:
        cst = ctx.enter_context(ExitStack())
        identb = cst.enter_context(nc.sbuf_tensor("identb", [128, 128], BF16))
        identf = cst.enter_context(nc.sbuf_tensor("identf", [128, 128], F32))
        maskneg = cst.enter_context(nc.sbuf_tensor("maskneg", [128, 128], F32))
        rotT = cst.enter_context(nc.sbuf_tensor("rotT", [64, 64], BF16))
        onesf = cst.enter_context(nc.sbuf_tensor("onesf", [128, 128], F32))
        lnsq = cst.enter_context(nc.sbuf_tensor("lnsq", [128, D], F32))
        sD.incd(Y_.dma_start(out=identb[:, :], in_=identb_d[:, :]))
        sD.incd(Y_.dma_start(out=identf[:, :], in_=identf_d[:, :]))
        sD.incd(Y_.dma_start(out=maskneg[:, :], in_=maskneg_d[:, :]))
        sD.incd(Y_.dma_start(out=rotT[:, :], in_=rotT_d[:, :]))
        sV.inc(V_.memset(onesf[:, :], 1.0))
        Y_.wait_ge(sD.h, sD.n)
        nc.all_engine_barrier()

        pers = ExitStack()
        kvcn = [pers.enter_context(nc.sbuf_tensor(f"kvcn{i}", [128, T], BF16))
                for i in range(2)]
        qcf = [pers.enter_context(nc.sbuf_tensor(f"qcf{j}", [128, T], BF16))
               for j in range(4)]


        pha = ExitStack()
        with pha:
            xnF = [pha.enter_context(nc.sbuf_tensor(f"xnF{c}", [128, T], BF16))
                   for c in range(8)]
            xt = pha.enter_context(nc.sbuf_tensor("xt", [128, D], F32))
            xnt = pha.enter_context(nc.sbuf_tensor("xnt", [128, D], F32))
            xnb = pha.enter_context(nc.sbuf_tensor("xnb", [128, D], BF16))
            st8 = pha.enter_context(nc.sbuf_tensor("st8", [128, 8], F32))
            wbuf = pha.enter_context(nc.sbuf_tensor("wbufA", [128, 4096], BF16))
            gkv = pha.enter_context(nc.sbuf_tensor("gkv", [128, 2], F32))
            sq2 = pha.enter_context(nc.sbuf_tensor("sq2", [128, 1024], F32))
            rs1 = pha.enter_context(nc.sbuf_tensor("rs1", [1, 512], F32))
            rbc = pha.enter_context(nc.sbuf_tensor("rbc", [128, 512], F32))
            cost = pha.enter_context(nc.sbuf_tensor("cost", [64, T], BF16))
            sint = pha.enter_context(nc.sbuf_tensor("sint", [64, T], BF16))
            krout = pha.enter_context(nc.sbuf_tensor("krout", [64, 512], BF16))
            krb = pha.enter_context(nc.sbuf_tensor("krb", [64, 512], BF16))
            rob = pha.enter_context(nc.sbuf_tensor("rob", [64, 512], BF16))
            tt1 = pha.enter_context(nc.sbuf_tensor("tt1", [64, 512], F32))
            tt2 = pha.enter_context(nc.sbuf_tensor("tt2", [64, 512], F32))

            pp = ExitStack()
            with pp:
                psA = pp.enter_context(nc.psum_tensor("psA", [128, 512], F32))
                psB = pp.enter_context(nc.psum_tensor("psB", [128, 512], F32))
                psC = pp.enter_context(nc.psum_tensor("psC", [128, 512], F32))
                psR = pp.enter_context(nc.psum_tensor("psR", [64, 512], F32))
                psS1 = pp.enter_context(nc.psum_tensor("psS1", [1, 512], F32))
                psT = [pp.enter_context(nc.psum_tensor(f"psT{i}", [128, 512], BF16))
                       for i in range(2)]

                sD.incd(Y_.dma_start(out=gkv[:, :], in_=gkv_pc[:, :]))
                sD.incd(Y_.dma_start(out=cost[:, :], in_=cos_d[:, :]))
                sD.incd(Y_.dma_start(out=sint[:, :], in_=sin_d[:, :]))
                for i in range(NT):
                    Y_.wait_ge(sV.h, sV.n)
                    sD.incd(Y_.dma_start(out=xt[:, :],
                                         in_=x_seq[i * 128:(i + 1) * 128, :]))
                    ln_tile(xt, xnt, st8, 1e-5)
                    V_.wait_ge(sV.h, sV.n)
                    sV.inc(V_.tensor_copy(xnb[:, :], xnt[:, :]))
                    for half in range(2):
                        ps = psT[half]
                        P_.wait_ge(sV.h, sV.n)
                        for j in range(4):
                            c = half * 4 + j
                            sPE.inc(P_.transpose(ps[:, j * 128:(j + 1) * 128],
                                                 xnb[:, c * 128:(c + 1) * 128],
                                                 identb[:, :]))
                        V_.wait_ge(sPE.h, sPE.n)
                        for j in range(4):
                            c = half * 4 + j
                            sV.inc(V_.tensor_copy(
                                xnF[c][:, i * 128:(i + 1) * 128],
                                ps[:, j * 128:(j + 1) * 128]))

                sG.incd(G_.dma_start(*_dz(wbuf[:, 0:2048], w_kv[:, :])))
                P_.wait_ge(sG.h, sG.n)
                for b in range(NB):
                    for i2 in range(2):
                        ps = psA if i2 == 0 else psB
                        P_.wait_ge(sV.h, sV.n)
                        for c in range(8):
                            sPE.inc(P_.matmul(
                                ps[:, :],
                                lhsT=wbuf[:, c * 256 + i2 * 128: c * 256 + (i2 + 1) * 128],
                                rhs=xnF[c][:, b * 512:(b + 1) * 512],
                                start=(c == 0), stop=(c == 7)))
                        A_.wait_ge(sPE.h, sPE.n)
                        sA.inc(A_.activation(sq2[:, i2 * 512:(i2 + 1) * 512],
                                             ps[:, :], AF.Square))
                    P_.wait_ge(sA.h, sA.n)
                    sPE.inc(P_.matmul(psS1[:, :], lhsT=onesf[:, 0:1],
                                      rhs=sq2[:, 0:512], start=True, stop=False))
                    sPE.inc(P_.matmul(psS1[:, :], lhsT=onesf[:, 0:1],
                                      rhs=sq2[:, 512:1024], start=False, stop=True))
                    V_.wait_ge(sPE.h, sPE.n)
                    sV.inc(V_.tensor_scalar(rs1[:, :], psS1[:, :],
                                            1.0 / KVR, 1e-6, ALU.mult, ALU.add))
                    A_.wait_ge(sV.h, sV.n)
                    sA.inc(A_.activation(rs1[:, :], rs1[:, :], AF.Ln))
                    A_.wait_ge(sA.h, sA.n)
                    sA.inc(A_.activation(rs1[:, :], rs1[:, :], AF.Exp, scale=-0.5))
                    P_.wait_ge(sA.h, sA.n)
                    sPE.inc(P_.matmul(psC[:, :], lhsT=onesf[0:1, 0:128],
                                      rhs=rs1[:, :], start=True, stop=True))
                    V_.wait_ge(sPE.h, sPE.n)
                    sV.inc(V_.tensor_copy(rbc[:, :], psC[:, :]))
                    V_.wait_ge(sV.h, sV.n)
                    V_.wait_ge(sD.h, sD.n)
                    for i2 in range(2):
                        ps = psA if i2 == 0 else psB
                        sV.inc(V_.scalar_tensor_tensor(
                            kvcn[i2][:, b * 512:(b + 1) * 512],
                            ps[:, :], gkv[:, i2:i2 + 1], rbc[:, :],
                            ALU.mult, ALU.mult))

                G_.wait_ge(sPE.h, sPE.n)
                sG.incd(G_.dma_start(*_dz(wbuf[:, :], w_q[:, :])))
                P_.wait_ge(sG.h, sG.n)
                for j in range(4):
                    for b in range(NB):
                        ps = psA if (b % 2 == 0) else psB
                        P_.wait_ge(sV.h, sV.n)
                        for c in range(8):
                            sPE.inc(P_.matmul(
                                ps[:, :],
                                lhsT=wbuf[:, c * 512 + j * 128: c * 512 + (j + 1) * 128],
                                rhs=xnF[c][:, b * 512:(b + 1) * 512],
                                start=(c == 0), stop=(c == 7)))
                        V_.wait_ge(sPE.h, sPE.n)
                        sV.inc(V_.tensor_copy(qcf[j][:, b * 512:(b + 1) * 512], ps[:, :]))

                kro_mark = 0
                G_.wait_ge(sPE.h, sPE.n)
                sG.incd(G_.dma_start(*_dz(wbuf[:, 0:2048], w_krope[:, :])))
                P_.wait_ge(sG.h, sG.n)
                V_.wait_ge(sD.h, sD.n)
                for p in range(4):
                    for b in range(NB):
                        P_.wait_ge(sA.h, sA.n)
                        for c in range(8):
                            sPE.inc(P_.matmul(
                                psR[:, :],
                                lhsT=wbuf[:, c * 256 + p * 64: c * 256 + (p + 1) * 64],
                                rhs=xnF[c][:, b * 512:(b + 1) * 512],
                                start=(c == 0), stop=(c == 7)))
                        A_.wait_ge(sPE.h, sPE.n)
                        A_.wait_ge(sV.h, sV.n)
                        sA.inc(A_.copy(krb[:, :], psR[:, :]))
                        P_.wait_ge(sA.h, sA.n)
                        sPE.inc(P_.matmul(psC[0:64, :], lhsT=rotT[:, :],
                                          rhs=krb[:, :], start=True, stop=True))
                        A_.wait_ge(sPE.h, sPE.n)
                        sA.inc(A_.copy(rob[:, :], psC[0:64, :]))
                        V_.wait_ge(sA.h, sA.n)
                        sV.inc(V_.tensor_mul(tt1[:, :], krb[:, :],
                                             cost[:, b * 512:(b + 1) * 512]))
                        sV.inc(V_.tensor_mul(tt2[:, :], rob[:, :],
                                             sint[:, b * 512:(b + 1) * 512]))
                        V_.wait_ge(sV.h, sV.n)
                        V_.wait_ge(sD.h, kro_mark)
                        sV.inc(V_.tensor_add(krout[:, :], tt1[:, :], tt2[:, :]))
                        Y_.wait_ge(sV.h, sV.n)
                        sD.incd(Y_.dma_start(
                            out=krope_dram[p * 64:(p + 1) * 64, b * 512:(b + 1) * 512],
                            in_=krout[:, :]))
                        kro_mark = sD.n

                G_.wait_ge(sPE.h, sPE.n)
                sG.incd(G_.dma_start(*_dz(wbuf[:, 0:1024], w_qrope[:, :])))
                P_.wait_ge(sG.h, sG.n)
                for p in range(4):
                    for b in range(NB):
                        P_.wait_ge(sA.h, sA.n)
                        for j in range(4):
                            sPE.inc(P_.matmul(
                                psR[:, :],
                                lhsT=wbuf[:, j * 256 + p * 64: j * 256 + (p + 1) * 64],
                                rhs=qcf[j][:, b * 512:(b + 1) * 512],
                                start=(j == 0), stop=(j == 3)))
                        A_.wait_ge(sPE.h, sPE.n)
                        A_.wait_ge(sV.h, sV.n)
                        sA.inc(A_.copy(krb[:, :], psR[:, :]))
                        P_.wait_ge(sA.h, sA.n)
                        sPE.inc(P_.matmul(psC[0:64, :], lhsT=rotT[:, :],
                                          rhs=krb[:, :], start=True, stop=True))
                        A_.wait_ge(sPE.h, sPE.n)
                        sA.inc(A_.copy(rob[:, :], psC[0:64, :]))
                        V_.wait_ge(sA.h, sA.n)
                        sV.inc(V_.tensor_mul(tt1[:, :], krb[:, :],
                                             cost[:, b * 512:(b + 1) * 512]))
                        sV.inc(V_.tensor_mul(tt2[:, :], rob[:, :],
                                             sint[:, b * 512:(b + 1) * 512]))
                        V_.wait_ge(sV.h, sV.n)
                        V_.wait_ge(sD.h, kro_mark)
                        sV.inc(V_.tensor_add(krout[:, :], tt1[:, :], tt2[:, :]))
                        Y_.wait_ge(sV.h, sV.n)
                        sD.incd(Y_.dma_start(
                            out=qrope_dram[p * 64:(p + 1) * 64, b * 512:(b + 1) * 512],
                            in_=krout[:, :]))
                        kro_mark = sD.n
                Y_.wait_ge(sD.h, sD.n)
                G_.wait_ge(sG.h, sG.n)
                nc.all_engine_barrier()

        att = ExitStack()
        with att:
            Kp = [att.enter_context(nc.sbuf_tensor(f"K{h}", [96, T], BF16))
                  for h in range(2)]
            Qp = [att.enter_context(nc.sbuf_tensor(f"Q{h}", [96, T], BF16))
                  for h in range(2)]
            Vp = [att.enter_context(nc.sbuf_tensor(f"V{h}", [128, NT * 64], BF16))
                  for h in range(2)]
            srow = att.enter_context(nc.sbuf_tensor("srow", [128, T], F32))
            prow = [att.enter_context(nc.sbuf_tensor(f"prow{i}", [128, T], BF16))
                    for i in range(2)]
            ptsb = [att.enter_context(nc.sbuf_tensor(f"ptsb{i}", [128, 512], BF16))
                    for i in range(2)]
            opair = att.enter_context(nc.sbuf_tensor("opair", [128, NT * 128], BF16))
            wdk = att.enter_context(nc.sbuf_tensor("wdk", [128, 1024], BF16))
            wdv = att.enter_context(nc.sbuf_tensor("wdv", [128, 1024], BF16))
            wdq = att.enter_context(nc.sbuf_tensor("wdq", [128, 2048], BF16))
            mrow = att.enter_context(nc.sbuf_tensor("mrow", [128, 8], F32))

            app = ExitStack()
            with app:
                psS = [app.enter_context(nc.psum_tensor(f"psSc{i}", [128, 512], F32))
                       for i in range(2)]
                psO = [app.enter_context(nc.psum_tensor(f"psO{i}", [128, 64], F32))
                       for i in range(2)]
                psPT = [app.enter_context(nc.psum_tensor(f"psPT{i}", [128, 512], BF16))
                        for i in range(2)]

                sG.incd(G_.dma_start(*_dz(wdk[:, :], w_kdec[:, :])))
                sG.incd(G_.dma_start(*_dz(wdv[:, :], w_vdec[:, :])))
                sG.incd(G_.dma_start(*_dz(wdq[:, :], w_qdec[:, :])))
                P_.wait_ge(sG.h, sG.n)

                o_dma_marks = []
                s_evac_mark = {0: 0, 1: 0}
                pt_copy_mark = {0: 0, 1: 0}
                o_evac_mark = {0: 0, 1: 0}
                pair_pe_mark = 0
                for pair in range(4):
                    Y_.wait_ge(sPE.h, pair_pe_mark)
                    sD.incd(Y_.dma_start(out=Kp[0][64:96, :],
                                         in_=krope_dram[pair * 64:pair * 64 + 32, :]))
                    sD.incd(Y_.dma_start(out=Kp[1][64:96, :],
                                         in_=krope_dram[pair * 64 + 32:pair * 64 + 64, :]))
                    sD.incd(Y_.dma_start(out=Qp[0][64:96, :],
                                         in_=qrope_dram[pair * 64:pair * 64 + 32, :]))
                    sD.incd(Y_.dma_start(out=Qp[1][64:96, :],
                                         in_=qrope_dram[pair * 64 + 32:pair * 64 + 64, :]))
                    rope_mark = sD.n
                    for b in range(NB):
                        ps = psS[0]
                        P_.wait_ge(sV.h, sV.n)
                        for i2 in range(2):
                            sPE.inc(P_.matmul(
                                ps[:, :],
                                lhsT=wdk[:, i2 * 512 + pair * 128: i2 * 512 + (pair + 1) * 128],
                                rhs=kvcn[i2][:, b * 512:(b + 1) * 512],
                                start=(i2 == 0), stop=(i2 == 1)))
                        V_.wait_ge(sPE.h, sPE.n)
                        sV.inc(V_.tensor_copy(Kp[0][0:64, b * 512:(b + 1) * 512],
                                              ps[0:64, :]))
                        sV.inc(V_.tensor_copy(Kp[1][0:64, b * 512:(b + 1) * 512],
                                              ps[64:128, :]))
                        ps = psS[1]
                        for j in range(4):
                            sPE.inc(P_.matmul(
                                ps[:, :],
                                lhsT=wdq[:, j * 512 + pair * 128: j * 512 + (pair + 1) * 128],
                                rhs=qcf[j][:, b * 512:(b + 1) * 512],
                                start=(j == 0), stop=(j == 3)))
                        V_.wait_ge(sPE.h, sPE.n)
                        sV.inc(V_.tensor_copy(Qp[0][0:64, b * 512:(b + 1) * 512],
                                              ps[0:64, :]))
                        sV.inc(V_.tensor_copy(Qp[1][0:64, b * 512:(b + 1) * 512],
                                              ps[64:128, :]))
                    for k in range(NT):
                        ps = psS[k % 2]
                        P_.wait_ge(sV.h, sV.n)
                        for i2 in range(2):
                            sPE.inc(P_.matmul(
                                ps[:, 0:128],
                                lhsT=kvcn[i2][:, k * 128:(k + 1) * 128],
                                rhs=wdv[:, i2 * 512 + pair * 128: i2 * 512 + (pair + 1) * 128],
                                start=(i2 == 0), stop=(i2 == 1)))
                        V_.wait_ge(sPE.h, sPE.n)
                        sV.inc(V_.tensor_copy(Vp[0][:, k * 64:(k + 1) * 64],
                                              ps[:, 0:64]))
                        sV.inc(V_.tensor_copy(Vp[1][:, k * 64:(k + 1) * 64],
                                              ps[:, 64:128]))

                    if o_dma_marks:
                        V_.wait_ge(sD.h, o_dma_marks[-1])

                    P_.wait_ge(sD.h, rope_mark)
                    build_v = sV.n

                    st = {}
                    last_exp = [0]

                    def s1_scores(t, hsel, it):
                        Kh, Qh = Kp[hsel], Qp[hsel]
                        L = (t + 1) * 128
                        nb = (L + 511) // 512
                        evm = 0
                        A_.wait_ge(sA.h, last_exp[0])
                        for b in range(nb):
                            w = min(512, L - b * 512)
                            ps = psS[b % 2]
                            P_.wait_ge(sA.h, s_evac_mark[b % 2])
                            if it == 0:
                                P_.wait_ge(sV.h, build_v)
                            sPE.inc(P_.matmul(
                                ps[:, 0:w],
                                lhsT=Qh[:, t * 128:(t + 1) * 128],
                                rhs=Kh[:, b * 512:b * 512 + w],
                                start=True, stop=True))
                            A_.wait_ge(sPE.h, sPE.n)
                            sA.inc(A_.copy(srow[:, b * 512:b * 512 + w],
                                           ps[:, 0:w]))
                            s_evac_mark[b % 2] = sA.n
                            evm = sA.n
                        V_.wait_ge(sA.h, evm)
                        sV.inc(V_.tensor_add(srow[:, t * 128:(t + 1) * 128],
                                             srow[:, t * 128:(t + 1) * 128],
                                             maskneg[:, :]))
                        V_.wait_ge(sV.h, sV.n)
                        pp = it % 2
                        sV.inc(V_.reduce_max(mrow[:, 6:7],
                                             srow[:, t * 128:(t + 1) * 128],
                                             axis=AX.X, negate=True))
                        V_.wait_ge(sV.h, sV.n)
                        sV.inc(V_.tensor_scalar_mul(mrow[:, pp:pp + 1],
                                                    mrow[:, 6:7], SCALE))
                        st[it] = dict(negm=sV.n)

                    def s2_exp(t, hsel, it):
                        L = (t + 1) * 128
                        pp = it % 2
                        pr = prow[pp]
                        A_.wait_ge(sV.h, st[it]["negm"])
                        sA.inc(A_.activation(pr[:, 0:L], srow[:, 0:L], AF.Exp,
                                             bias=mrow[:, pp:pp + 1], scale=SCALE,
                                             accum_out=mrow[:, 2 + pp:3 + pp]))
                        st[it]["exp"] = sA.n
                        last_exp[0] = sA.n
                        V_.wait_ge(sA.h, sA.n)
                        sV.inc(V_.reciprocal(mrow[:, 4 + pp:5 + pp],
                                             mrow[:, 2 + pp:3 + pp]))
                        st[it]["rl"] = sV.n

                    def s3_pv(t, hsel, it):
                        Vh = Vp[hsel]
                        pp = it % 2
                        pr = prow[pp]
                        po = psO[pp]
                        ktile = 0
                        first = True
                        while ktile <= t:
                            kn = min(4, t + 1 - ktile)
                            buf = (ktile // 4) % 2
                            if first:
                                P_.wait_ge(sA.h, st[it]["exp"])
                            P_.wait_ge(sV.h, pt_copy_mark[buf])
                            for j in range(kn):
                                sPE.inc(P_.transpose(
                                    psPT[buf][:, j * 128:(j + 1) * 128],
                                    pr[:, (ktile + j) * 128:(ktile + j + 1) * 128],
                                    identb[:, :]))
                            V_.wait_ge(sPE.h, sPE.n)
                            sV.inc(V_.tensor_copy(ptsb[buf][:, 0:kn * 128],
                                                  psPT[buf][:, 0:kn * 128]))
                            pt_copy_mark[buf] = sV.n
                            P_.wait_ge(sV.h, pt_copy_mark[buf])
                            if first:
                                P_.wait_ge(sV.h, o_evac_mark[pp])
                                first = False
                            for j in range(kn):
                                kk = ktile + j
                                sPE.inc(P_.matmul(
                                    po[:, :],
                                    lhsT=ptsb[buf][:, j * 128:(j + 1) * 128],
                                    rhs=Vh[:, kk * 64:(kk + 1) * 64],
                                    start=(kk == 0), stop=(kk == t)))
                            ktile += kn
                        V_.wait_ge(sPE.h, sPE.n)
                        V_.wait_ge(sV.h, st[it]["rl"])
                        sV.inc(V_.tensor_scalar_mul(
                            opair[:, t * 128 + hsel * 64: t * 128 + hsel * 64 + 64],
                            po[:, :], mrow[:, 4 + pp:5 + pp]))
                        o_evac_mark[pp] = sV.n
                        del st[it]

                    iters = [(t, hsel) for t in range(NT) for hsel in range(2)]
                    prev = None
                    for it, (t, hsel) in enumerate(iters):
                        s1_scores(t, hsel, it)
                        if prev is None:
                            s2_exp(t, hsel, it)
                        else:
                            s3_pv(*prev)
                            s2_exp(t, hsel, it)
                        prev = (t, hsel, it)
                    s3_pv(*prev)
                    pair_pe_mark = sPE.n

                    Y_.wait_ge(sV.h, sV.n)
                    for t in range(NT):
                        sD.incd(Y_.dma_start(
                            out=o_dram[t * 128:(t + 1) * 128,
                                       pair * 128:(pair + 1) * 128],
                            in_=opair[:, t * 128:(t + 1) * 128]))
                    o_dma_marks.append(sD.n)

                G_.wait_ge(sD.h, sD.n)
                sCC.inc(G_.collective_compute(
                    "AllGather", ALU.bypass,
                    replica_groups=[[0, 4], [1, 5], [2, 6], [3, 7]],
                    ins=[o_dram[:, :].opt()],
                    outs=[og_dram[:, :].opt()]))
                nc.all_engine_barrier()

        pers.close()


        ffn = ExitStack()
        with ffn:
            cent = ffn.enter_context(nc.sbuf_tensor("cent", [128, 64], F32))
            lngs = ffn.enter_context(nc.sbuf_tensor("lngs", [128, 64], F32))
            lnbb = ffn.enter_context(nc.sbuf_tensor("lnbb", [128, 64], BF16))
            b1s = ffn.enter_context(nc.sbuf_tensor("b1s", [128, 256], F32))
            b2s = ffn.enter_context(nc.sbuf_tensor("b2s", [128, 64], F32))
            call = ffn.enter_context(nc.sbuf_tensor("call", [128, 128], F32))
            ohb = ffn.enter_context(nc.sbuf_tensor("ohb", [128, 128], BF16))
            ohf = ffn.enter_context(nc.sbuf_tensor("ohf", [128, 128], F32))
            rkal = ffn.enter_context(nc.sbuf_tensor("rkal", [128, 128], F32))
            trow = ffn.enter_context(nc.sbuf_tensor("trow", [1, 384], F32))
            cselsb = ffn.enter_context(nc.sbuf_tensor("cselsb", [128, NOT], F32))
            sloti = ffn.enter_context(nc.sbuf_tensor("sloti", [128, NOT], I32))
            ltri = ffn.enter_context(nc.sbuf_tensor("ltri", [128, 128], BF16))
            basev = ffn.enter_context(nc.sbuf_tensor("basev", [1, 8], F32))
            onesb1 = ffn.enter_context(nc.sbuf_tensor("onesb1", [128, 1], BF16))
            sD.incd(Y_.dma_start(out=ltri[:, :], in_=ltri_d[:, :]))
            sD.incd(Y_.dma_start(out=basev[:, :], in_=basev_d[:, :]))
            sV.inc(V_.memset(onesb1[:, :], 1.0))

            rt = ExitStack()
            with rt:
                xt2 = rt.enter_context(nc.sbuf_tensor("xt2", [128, D], F32))
                xnt2 = rt.enter_context(nc.sbuf_tensor("xnt2", [128, D], F32))
                xht = rt.enter_context(nc.sbuf_tensor("xht", [128, D], F32))
                xhb = rt.enter_context(nc.sbuf_tensor("xhb", [128, D], BF16))
                xnTs = rt.enter_context(nc.sbuf_tensor("xnTs", [128, 1024], F32))
                st8b = rt.enter_context(nc.sbuf_tensor("st8b", [128, 8], F32))
                st8c = rt.enter_context(nc.sbuf_tensor("st8c", [128, 8], F32))
                affs = rt.enter_context(nc.sbuf_tensor("affs", [128, 8], F32))
                sgs = rt.enter_context(nc.sbuf_tensor("sgs", [128, 8], F32))
                mx1 = rt.enter_context(nc.sbuf_tensor("mx1", [128, 4], F32))
                tk = rt.enter_context(nc.sbuf_tensor("tk", [128, 3], F32))
                bc8 = rt.enter_context(nc.sbuf_tensor("bc8", [128, 8], F32))
                tmp8 = rt.enter_context(nc.sbuf_tensor("tmp8", [128, 8], F32))

                rpp = ExitStack()
                with rpp:
                    psTr = rpp.enter_context(nc.psum_tensor("psTr", [128, 512], F32))
                    psAf = rpp.enter_context(nc.psum_tensor("psAf", [128, 8], F32))
                    psRk = rpp.enter_context(nc.psum_tensor("psRk", [128, 8], F32))
                    psTot = rpp.enter_context(nc.psum_tensor("psTot", [1, 8], F32))
                    psBc = rpp.enter_context(nc.psum_tensor("psBc", [128, 8], F32))

                    sD.incd(Y_.dma_start(out=cent[:, :], in_=cent_pc[:, :]))
                    sD.incd(Y_.dma_start(out=lngs[:, :], in_=lng_pc[:, :]))
                    sG.incd(G_.dma_start(out=lnbb[:, :], in_=lnb_pc[:, :]))
                    sD.incd(Y_.dma_start(out=b1s[:, :], in_=b1_pc[:, :]))
                    sD.incd(Y_.dma_start(out=b2s[:, :], in_=b2_pc[:, :]))

                    for i in range(NOT):
                        Y_.wait_ge(sV.h, sV.n)
                        sD.incd(Y_.dma_start(out=xt2[:, :],
                                             in_=x_own[i * 128:(i + 1) * 128, :]))
                        ln_tile(xt2, xnt2, st8b, 1e-5)
                        V_.wait_ge(sV.h, sV.n)
                        sV.inc(V_.reduce_sum(st8c[:, 0:1], xnt2[:, :], axis=AX.X))
                        A_.wait_ge(sV.h, sV.n)
                        sA.inc(A_.activation(lnsq[:, :], xnt2[:, :], AF.Square,
                                             accum_out=st8c[:, 1:2]))
                        V_.wait_ge(sV.h, sV.n)
                        sV.inc(V_.tensor_scalar_mul(st8c[:, 2:3], st8c[:, 0:1], 1.0 / D))
                        V_.wait_ge(sA.h, sA.n)
                        V_.wait_ge(sV.h, sV.n)
                        sV.inc(V_.tensor_scalar_mul(st8c[:, 3:4], st8c[:, 1:2], 1.0 / D))
                        V_.wait_ge(sV.h, sV.n)
                        sV.inc(V_.tensor_mul(st8c[:, 4:5], st8c[:, 2:3], st8c[:, 2:3]))
                        V_.wait_ge(sV.h, sV.n)
                        sV.inc(V_.tensor_sub(st8c[:, 5:6], st8c[:, 3:4], st8c[:, 4:5]))
                        V_.wait_ge(sV.h, sV.n)
                        sV.inc(V_.tensor_scalar_add(st8c[:, 6:7], st8c[:, 5:6], 1e-5))
                        A_.wait_ge(sV.h, sV.n)
                        sA.inc(A_.activation(st8c[:, 7:8], st8c[:, 6:7], AF.Ln))
                        A_.wait_ge(sA.h, sA.n)
                        sA.inc(A_.activation(st8c[:, 6:7], st8c[:, 7:8], AF.Exp, scale=-0.5))
                        V_.wait_ge(sA.h, sA.n)
                        sV.inc(V_.tensor_scalar(xht[:, :], xnt2[:, :], st8c[:, 2:3],
                                                st8c[:, 6:7], ALU.subtract, ALU.mult))
                        V_.wait_ge(sV.h, sV.n)
                        sV.inc(V_.tensor_copy(xhb[:, :], xht[:, :]))
                        Y_.wait_ge(sV.h, sV.n)
                        sD.incd(Y_.dma_start(
                            out=xhtm_dram[i * 128:(i + 1) * 128, :], in_=xhb[:, :]))
                        for half in range(2):
                            P_.wait_ge(sV.h, sV.n)
                            for j in range(4):
                                c = half * 4 + j
                                sPE.inc(P_.transpose(psTr[:, j * 128:(j + 1) * 128],
                                                     xnt2[:, c * 128:(c + 1) * 128],
                                                     identf[:, :]))
                            V_.wait_ge(sPE.h, sPE.n)
                            sV.inc(V_.tensor_copy(xnTs[:, half * 512:(half + 1) * 512],
                                                  psTr[:, :]))
                        P_.wait_ge(sV.h, sV.n)
                        P_.wait_ge(sD.h, sD.n)
                        for c in range(8):
                            sPE.inc(P_.matmul(psAf[:, :],
                                              lhsT=xnTs[:, c * 128:(c + 1) * 128],
                                              rhs=cent[:, c * 8:(c + 1) * 8],
                                              start=(c == 0), stop=(c == 7)))
                        V_.wait_ge(sPE.h, sPE.n)
                        sV.inc(V_.tensor_copy(affs[:, :], psAf[:, :]))
                        A_.wait_ge(sV.h, sV.n)
                        sA.inc(A_.activation(sgs[:, :], affs[:, :], AF.Sigmoid))
                        V_.wait_ge(sV.h, sV.n)
                        sV.inc(V_.reduce_max(mx1[:, 0:1], affs[:, :], axis=AX.X))
                        sV.inc(V_.memset(tk[:, 0:1], 0.0))
                        V_.wait_ge(sA.h, sA.n)
                        for e in range(E):
                            V_.wait_ge(sV.h, sV.n)
                            sV.inc(V_.tensor_tensor(tk[:, 1:2], affs[:, e:e + 1],
                                                    mx1[:, 0:1], ALU.is_equal))
                            V_.wait_ge(sV.h, sV.n)
                            sV.inc(V_.tensor_scalar(tk[:, 2:3], tk[:, 0:1], -1.0, 1.0,
                                                    ALU.mult, ALU.add))
                            V_.wait_ge(sV.h, sV.n)
                            sV.inc(V_.tensor_mul(tk[:, 1:2], tk[:, 1:2], tk[:, 2:3]))
                            V_.wait_ge(sV.h, sV.n)
                            sV.inc(V_.tensor_add(tk[:, 0:1], tk[:, 0:1], tk[:, 1:2]))
                            sV.inc(V_.tensor_mul(call[:, i * 8 + e:i * 8 + e + 1],
                                                 tk[:, 1:2], sgs[:, e:e + 1]))
                            sV.inc(V_.tensor_copy(ohb[:, i * 8 + e:i * 8 + e + 1],
                                                  tk[:, 1:2]))
                            sV.inc(V_.tensor_copy(ohf[:, i * 8 + e:i * 8 + e + 1],
                                                  tk[:, 1:2]))
                        V_.wait_ge(sV.h, sV.n)
                        sV.inc(V_.reduce_sum(cselsb[:, i:i + 1],
                                             call[:, i * 8:(i + 1) * 8], axis=AX.X))
                        P_.wait_ge(sV.h, sV.n)
                        sPE.inc(P_.matmul(psRk[:, :], lhsT=ltri[:, :],
                                          rhs=ohb[:, i * 8:(i + 1) * 8],
                                          start=True, stop=True))
                        sPE.inc(P_.matmul(psTot[:, :], lhsT=onesb1[:, 0:1],
                                          rhs=ohb[:, i * 8:(i + 1) * 8],
                                          start=True, stop=True))
                        V_.wait_ge(sPE.h, sPE.n)
                        sV.inc(V_.tensor_copy(rkal[:, i * 8:(i + 1) * 8], psRk[:, :]))
                        sV.inc(V_.tensor_copy(trow[0:1, i * 8:(i + 1) * 8], psTot[:, :]))

                    sV.inc(V_.memset(trow[0:1, 128:136], 0.0))
                    for i in range(1, NOT):
                        V_.wait_ge(sV.h, sV.n)
                        sV.inc(V_.tensor_add(trow[0:1, 128 + i * 8:136 + i * 8],
                                             trow[0:1, 120 + i * 8:128 + i * 8],
                                             trow[0:1, (i - 1) * 8:i * 8]))
                    V_.wait_ge(sD.h, sD.n)
                    for i in range(NOT):
                        V_.wait_ge(sV.h, sV.n)
                        sV.inc(V_.tensor_add(trow[0:1, 256:264],
                                             trow[0:1, 128 + i * 8:136 + i * 8],
                                             basev[0:1, :]))
                        P_.wait_ge(sV.h, sV.n)
                        sPE.inc(P_.matmul(psBc[:, :], lhsT=onesf[0:1, 0:128],
                                          rhs=trow[0:1, 256:264], start=True, stop=True))
                        V_.wait_ge(sPE.h, sPE.n)
                        sV.inc(V_.tensor_copy(bc8[:, :], psBc[:, :]))
                        V_.wait_ge(sV.h, sV.n)
                        sV.inc(V_.tensor_add(tmp8[:, :], rkal[:, i * 8:(i + 1) * 8],
                                             bc8[:, :]))
                        V_.wait_ge(sV.h, sV.n)
                        sV.inc(V_.tensor_mul(tmp8[:, :], tmp8[:, :],
                                             ohf[:, i * 8:(i + 1) * 8]))
                        V_.wait_ge(sV.h, sV.n)
                        sV.inc(V_.reduce_sum(mx1[:, 1:2], tmp8[:, :], axis=AX.X))
                        V_.wait_ge(sV.h, sV.n)
                        sV.inc(V_.tensor_copy(sloti[:, i:i + 1], mx1[:, 1:2]))
                    sV.inc(V_.memset(xhb[:, :], 0.0))
                    G_.wait_ge(sV.h, sV.n)
                    for e in range(E):
                        for g in range(4):
                            sG.incd(G_.dma_start(
                                out=xg_dram[e * GL + g * 128:e * GL + (g + 1) * 128, :],
                                in_=xhb[:, :]))
                    G_.wait_ge(sG.h, sG.n)
                    G_.wait_ge(sD.h, sD.n)
                    for i in range(NOT):
                        Y_.wait_ge(sG.h, sG.n)
                        sD.incd(Y_.dma_start(out=xhb[:, :],
                                             in_=xhtm_dram[i * 128:(i + 1) * 128, :]))
                        G_.wait_ge(sD.h, sD.n)
                        G_.wait_ge(sV.h, sV.n)
                        sG.incd(G_.indirect_dma_start(
                            xg_dram[:, :], IndirectOffsetOnAxis(sloti[:, i:i + 1], 0),
                            xhb[:, :], None,
                            bounds_check=E * GL - 1, oob_is_err=False))
                        G_.wait_ge(sG.h, sG.n)
                    for e in range(E):
                        for g in range(12):
                            sG.incd(G_.dma_start(
                                out=dec_dram[e * GL + 512 + g * 128:
                                             e * GL + 512 + (g + 1) * 128, :],
                                in_=xhb[:, :]))
                    nc.all_engine_barrier()

            spx = ExitStack()
            with spx:
                gsb = [spx.enter_context(nc.sbuf_tensor(f"gsb{h}", [128, 512], BF16))
                       for h in range(2)]
                gfs = [spx.enter_context(nc.sbuf_tensor(f"gfs{h}", [128, 512], BF16))
                       for h in range(2)]
                wos = [spx.enter_context(nc.sbuf_tensor(f"wos{h}", [128, 4096], BF16))
                       for h in range(2)]
                idsb = [spx.enter_context(nc.sbuf_tensor(f"idsb{h}", [128, NOT], I32))
                        for h in range(2)]
                stx = spx.enter_context(nc.sbuf_tensor("stx", [128, 1024], F32))

                spp = ExitStack()
                with spp:
                    psG0 = spp.enter_context(nc.psum_tensor("psG0", [128, 512], BF16))
                    psSP = [spp.enter_context(nc.psum_tensor(f"psSP{i}", [128, 512], F32))
                            for i in range(2)]

                    sG.incd(G_.dma_start(*_dz(wos[0][:, :], wo_a[:, :])))
                    sG.incd(G_.dma_start(*_dz(wos[1][:, :], wo_b[:, :])))
                    sD.incd(Y_.dma_start(out=idsb[0][:, :], in_=ids_a_d[:, :]))
                    sD.incd(Y_.dma_start(out=idsb[1][:, :], in_=ids_b_d[:, :]))
                    G_.wait_ge(sCC.h, 1)
                    G_.wait_ge(sD.h, sD.n)
                    for i in range(NOT):
                        for h in range(2):
                            G_.wait_ge(sPE.h, sPE.n)
                            sG.incd(G_.indirect_dma_start(
                                gsb[h][:, :], None,
                                og_dram[:, :], IndirectOffsetOnAxis(idsb[h][:, i:i + 1], 0)))
                        P_.wait_ge(sG.h, sG.n)
                        for h in range(2):
                            P_.wait_ge(sV.h, sV.n)
                            for ch in range(4):
                                sPE.inc(P_.transpose(psG0[:, ch * 128:(ch + 1) * 128],
                                                     gsb[h][:, ch * 128:(ch + 1) * 128],
                                                     identb[:, :]))
                            V_.wait_ge(sPE.h, sPE.n)
                            sV.inc(V_.tensor_copy(gfs[h][:, :], psG0[:, :]))
                        P_.wait_ge(sV.h, sV.n)
                        for oc in range(2):
                            ps = psSP[oc]
                            P_.wait_ge(sA.h, sA.n)
                            for h in range(2):
                                for ch in range(4):
                                    sPE.inc(P_.matmul(
                                        ps[:, :],
                                        lhsT=gfs[h][:, ch * 128:(ch + 1) * 128],
                                        rhs=wos[h][:, ch * 1024 + oc * 512:
                                                   ch * 1024 + (oc + 1) * 512],
                                        start=(h == 0 and ch == 0),
                                        stop=(h == 1 and ch == 3)))
                            A_.wait_ge(sPE.h, sPE.n)
                            sA.inc(A_.copy(stx[:, oc * 512:(oc + 1) * 512], ps[:, :]))
                        Y_.wait_ge(sA.h, sA.n)
                        sD.incd(Y_.dma_start(out=s_dram[i * 128:(i + 1) * 128, :],
                                             in_=stx[:, :]))
                    Y_.wait_ge(sD.h, sD.n)
                    nc.all_engine_barrier()
            exs = ExitStack()
            with exs:
                w1sb = exs.enter_context(nc.sbuf_tensor("w1sb", [128, 32768], BF16))
                w2sb = exs.enter_context(nc.sbuf_tensor("w2sb", [128, 32768], BF16))
                hsb = exs.enter_context(nc.sbuf_tensor("hsb", [128, 16384], BF16))
                xgF = exs.enter_context(nc.sbuf_tensor("xgF", [128, 4096], BF16))
                gth = hsb

                bfs = exs.enter_context(nc.sbuf_tensor("bfs", [128, 32], F32))
                stgb = exs.enter_context(nc.sbuf_tensor("stgb", [128, 512], BF16))
                stg2 = [exs.enter_context(nc.sbuf_tensor(f"stg2_{j}", [128, 1024], BF16))
                        for j in range(4)]

                epp = ExitStack()
                with epp:
                    psH = [epp.enter_context(nc.psum_tensor(f"psH{i}", [128, 512], F32))
                           for i in range(2)]
                    psDc = [epp.enter_context(nc.psum_tensor(f"psDc{i}", [128, 512], F32))
                            for i in range(2)]
                    psBi = epp.enter_context(nc.psum_tensor("psBi", [128, 32], F32))
                    psTT = [epp.enter_context(nc.psum_tensor(f"psTT{i}", [128, 512], BF16))
                            for i in range(2)]

                    h_done_pe = 0
                    dec_done_pe = 0
                    gth_pe_mark = 0
                    relu_mark = {0: 0, 1: 0}
                    stt_mark = {0: 0, 1: 0}
                    ttc_mark = {0: 0, 1: 0}
                    scat_mark = 0
                    for e in range(E):
                        G_.wait_ge(sPE.h, h_done_pe)
                        sG.incd(G_.dma_start(*_dz(w1sb[:, :], w1t[e, :, :])))
                        w1_mark = sG.n
                        G_.wait_ge(sPE.h, dec_done_pe)
                        sG.incd(G_.dma_start(*_dz(w2sb[:, :], w2t[e, :, :])))
                        w2_mark = sG.n
                        G_.wait_ge(sG.h, scat_mark)
                        G_.wait_ge(sPE.h, gth_pe_mark)
                        G_.wait_ge(sA.h, relu_mark[0])
                        G_.wait_ge(sA.h, relu_mark[1])
                        for g in range(4):
                            sG.incd(G_.dma_start(
                                out=gth[:, g * 1024:(g + 1) * 1024],
                                in_=xg_dram[e * GL + g * 128:e * GL + (g + 1) * 128, :]))
                        gth_mark = sG.n
                        P_.wait_ge(sG.h, gth_mark)
                        for g in range(4):
                            for half in range(2):
                                pst = psTT[half]
                                P_.wait_ge(sV.h, ttc_mark[half])
                                for j in range(4):
                                    c = half * 4 + j
                                    sPE.inc(P_.transpose(
                                        pst[:, j * 128:(j + 1) * 128],
                                        gth[:, g * 1024 + c * 128: g * 1024 + (c + 1) * 128],
                                        identb[:, :]))
                                tmark = sPE.n
                                V_.wait_ge(sPE.h, tmark)
                                xgv = xgF[:, :].rearrange("p (c t) -> p c t", c=8)
                                sV.inc(V_.tensor_copy(
                                    xgv[:, half * 4:(half + 1) * 4,
                                        g * 128:(g + 1) * 128],
                                    pst[:, :].rearrange("p (j t) -> p j t", j=4)))
                                ttc_mark[half] = sV.n
                        gth_pe_mark = sPE.n
                        V_.wait_ge(sV.h, sV.n)
                        for c in range(8):
                            sV.inc(V_.tensor_scalar_mul(
                                xgF[:, c * 512:(c + 1) * 512],
                                xgF[:, c * 512:(c + 1) * 512],
                                lngs[:, e * 8 + c: e * 8 + c + 1]))
                        xg_scaled = sV.n
                        P_.wait_ge(sG.h, w1_mark)
                        P_.wait_ge(sV.h, xg_scaled)
                        bfs_mark = 0
                        for fc in range(32):
                            ps = psH[fc % 2]
                            P_.wait_ge(sA.h, relu_mark[fc % 2])
                            if fc > 0:
                                P_.wait_ge(sV.h, bfs_mark)
                            for c in range(8):
                                sPE.inc(P_.matmul(
                                    ps[:, :],
                                    lhsT=w1sb[:, c * 4096 + fc * 128:
                                              c * 4096 + (fc + 1) * 128],
                                    rhs=xgF[:, c * 512:(c + 1) * 512],
                                    start=(c == 0), stop=(c == 7)))
                                sPE.inc(P_.matmul(
                                    psBi[:, fc:fc + 1],
                                    lhsT=w1sb[:, c * 4096 + fc * 128:
                                              c * 4096 + (fc + 1) * 128],
                                    rhs=lnbb[:, e * 8 + c: e * 8 + c + 1],
                                    start=(c == 0), stop=(c == 7)))
                            mm_mark = sPE.n
                            V_.wait_ge(sPE.h, mm_mark)
                            sV.inc(V_.scalar_tensor_tensor(
                                bfs[:, fc:fc + 1], psBi[:, fc:fc + 1], 1.0,
                                b1s[:, e * 32 + fc: e * 32 + fc + 1],
                                ALU.mult, ALU.add))
                            bfs_mark = sV.n
                            A_.wait_ge(sV.h, bfs_mark)
                            A_.wait_ge(sPE.h, mm_mark)
                            sA.inc(A_.activation(hsb[:, fc * 512:(fc + 1) * 512],
                                                 ps[:, :], AF.Relu,
                                                 bias=bfs[:, fc:fc + 1]))
                            relu_mark[fc % 2] = sA.n
                        h_done_pe = sPE.n
                        relu_last = sA.n
                        P_.wait_ge(sA.h, relu_last)
                        P_.wait_ge(sG.h, w2_mark)
                        for dc in range(8):
                            ps = psDc[dc % 2]
                            P_.wait_ge(sV.h, stt_mark[dc % 2])
                            for fc in range(32):
                                sPE.inc(P_.matmul(
                                    ps[:, :],
                                    lhsT=w2sb[:, fc * 1024 + dc * 128:
                                              fc * 1024 + (dc + 1) * 128],
                                    rhs=hsb[:, fc * 512:(fc + 1) * 512],
                                    start=(fc == 0), stop=(fc == 31)))
                            dec_mm = sPE.n
                            V_.wait_ge(sPE.h, dec_mm)
                            sV.inc(V_.tensor_scalar(
                                stgb[:, :], ps[:, :],
                                b2s[:, e * 8 + dc: e * 8 + dc + 1], None,
                                ALU.add))
                            stt_mark[dc % 2] = sV.n
                            pst = psTT[dc % 2]
                            P_.wait_ge(sV.h, stt_mark[dc % 2])
                            P_.wait_ge(sV.h, ttc_mark[dc % 2])
                            for j in range(4):
                                sPE.inc(P_.transpose(pst[:, j * 128:(j + 1) * 128],
                                                     stgb[:, j * 128:(j + 1) * 128],
                                                     identb[:, :]))
                            tt_pe = sPE.n
                            V_.wait_ge(sPE.h, tt_pe)
                            V_.wait_ge(sG.h, scat_mark)
                            for j in range(4):
                                sV.inc(V_.tensor_copy(
                                    stg2[j][:, dc * 128:(dc + 1) * 128],
                                    pst[:, j * 128:(j + 1) * 128]))
                            ttc_mark[dc % 2] = sV.n
                        dec_done_pe = sPE.n
                        G_.wait_ge(sV.h, sV.n)
                        for j in range(4):
                            sG.incd(G_.dma_start(
                                out=dec_dram[e * GL + j * 128:e * GL + (j + 1) * 128, :],
                                in_=stg2[j][:, :]))
                        scat_mark = sG.n
                    G_.wait_ge(sG.h, sG.n)
                    nc.all_engine_barrier()


            eps = ExitStack()
            with eps:
                xt3 = eps.enter_context(nc.sbuf_tensor("xt3", [128, D], F32))
                xnt3 = eps.enter_context(nc.sbuf_tensor("xnt3", [128, D], F32))
                acct = eps.enter_context(nc.sbuf_tensor("acct", [128, D], F32))
                decg = eps.enter_context(nc.sbuf_tensor("decg", [128, D], BF16))
                stt3 = eps.enter_context(nc.sbuf_tensor("stt3", [128, D], F32))
                gt3 = eps.enter_context(nc.sbuf_tensor("gt3", [128, D], F32))
                ot3 = eps.enter_context(nc.sbuf_tensor("ot3", [128, D], F32))
                st8d = eps.enter_context(nc.sbuf_tensor("st8d", [128, 8], F32))

                Y_.wait_ge(sG.h, sG.n)
                dec_gather_v = 0
                for i in range(NOT):
                    Y_.wait_ge(sV.h, sV.n)
                    Y_.wait_ge(sD.h, sD.n)
                    sD.incd(Y_.dma_start(out=xt3[:, :],
                                         in_=x_own[i * 128:(i + 1) * 128, :]))
                    G_.wait_ge(sV.h, dec_gather_v)
                    sG.incd(G_.indirect_dma_start(
                        decg[:, :], None,
                        dec_dram[:, :], IndirectOffsetOnAxis(sloti[:, i:i + 1], 0),
                        bounds_check=E * GL - 1, oob_is_err=False))
                    V_.wait_ge(sG.h, sG.n)
                    sV.inc(V_.tensor_copy(acct[:, :], decg[:, :]))
                    dec_gather_v = sV.n
                    sD.incd(Y_.dma_start(out=stt3[:, :],
                                         in_=s_dram[i * 128:(i + 1) * 128, :]))
                    ln_tile(xt3, xnt3, st8d, 1e-5)
                    V_.wait_ge(sV.h, sV.n)
                    V_.wait_ge(sD.h, sD.n)
                    sV.inc(V_.scalar_tensor_tensor(
                        gt3[:, :], acct[:, :], cselsb[:, i:i + 1], xnt3[:, :],
                        ALU.mult, ALU.add))
                    A_.wait_ge(sV.h, sV.n)
                    sA.inc(A_.activation(xnt3[:, :], gt3[:, :], AF.Silu))
                    V_.wait_ge(sA.h, sA.n)
                    sV.inc(V_.tensor_mul(gt3[:, :], xnt3[:, :], stt3[:, :]))
                    V_.wait_ge(sV.h, sV.n)
                    sV.inc(V_.tensor_add(ot3[:, :], gt3[:, :], xt3[:, :]))
                    Y_.wait_ge(sV.h, sV.n)
                    sD.incd(Y_.dma_start(out=out_own[i * 128:(i + 1) * 128, :],
                                         in_=ot3[:, :]))
                Y_.wait_ge(sD.h, sD.n)
    return nc


_NC_CACHE = {}


def kernel(L0_attn_x, L0_attn_W_kv, L0_attn_g_kv, L0_attn_W_kdec, L0_attn_W_vdec,
           L0_attn_W_q, L0_attn_W_qdec, L0_attn_W_krope, L0_attn_W_qrope,
           L0_attn_W_o, L0_ffn_centroids, L0_ffn_ln_g, L0_ffn_ln_b, L0_ffn_w1,
           L0_ffn_b1, L0_ffn_w2, L0_ffn_b2):
    f32 = np.float32
    x = np.ascontiguousarray(np.asarray(L0_attn_x, dtype=f32))
    wkv = np.ascontiguousarray(np.asarray(L0_attn_W_kv, dtype=f32))
    gkv = np.asarray(L0_attn_g_kv, dtype=f32)
    wkdec = np.asarray(L0_attn_W_kdec, dtype=f32)
    wvdec = np.asarray(L0_attn_W_vdec, dtype=f32)
    wq = np.ascontiguousarray(np.asarray(L0_attn_W_q, dtype=f32))
    wqdec = np.asarray(L0_attn_W_qdec, dtype=f32)
    wkrope = np.asarray(L0_attn_W_krope, dtype=f32)
    wqrope = np.asarray(L0_attn_W_qrope, dtype=f32)
    wo = np.asarray(L0_attn_W_o, dtype=f32)
    cent = np.asarray(L0_ffn_centroids, dtype=f32)
    lng = np.asarray(L0_ffn_ln_g, dtype=f32)
    lnb = np.asarray(L0_ffn_ln_b, dtype=f32)
    w1 = np.asarray(L0_ffn_w1, dtype=f32)
    b1 = np.asarray(L0_ffn_b1, dtype=f32)
    w2 = np.asarray(L0_ffn_w2, dtype=f32)
    b2 = np.asarray(L0_ffn_b2, dtype=f32)

    gkv_pc = np.ascontiguousarray(gkv.reshape(2, 128).T)
    cent_pc = np.ascontiguousarray(
        cent.T.reshape(8, 128, 8).transpose(1, 0, 2).reshape(128, 64))
    lng_pc = np.ascontiguousarray(
        lng.reshape(8, 8, 128).transpose(2, 0, 1).reshape(128, 64))
    lnb_pc = np.ascontiguousarray(
        lnb.reshape(8, 8, 128).transpose(2, 0, 1).reshape(128, 64))
    b1_pc = np.ascontiguousarray(
        b1.reshape(8, 32, 128).transpose(2, 0, 1).reshape(128, 256))
    b2_pc = np.ascontiguousarray(
        b2.reshape(8, 8, 128).transpose(2, 0, 1).reshape(128, 64))
    w1t = np.ascontiguousarray(w1.transpose(0, 2, 1))
    w2t = np.ascontiguousarray(w2.transpose(0, 2, 1))

    in_maps = []
    for c in range(8):
        seq = c & 3
        hh = c >> 2
        hs = hh * 512
        hr = hh * 256
        my_rank = 0 if c < 4 else 1
        own = hh * OWN + np.arange(OWN)
        ids_a = (my_rank * T + own).reshape(NOT, 128).T.astype(np.int32)
        ids_b = ((1 - my_rank) * T + own).reshape(NOT, 128).T.astype(np.int32)
        in_maps.append({
            "x_seq": np.ascontiguousarray(x[seq]),
            "x_own": np.ascontiguousarray(x[seq, hh * OWN:(hh + 1) * OWN]),
            "w_kv": wkv,
            "gkv_pc": gkv_pc,
            "w_kdec": np.ascontiguousarray(wkdec[:, hs:hs + 512]),
            "w_vdec": np.ascontiguousarray(wvdec[:, hs:hs + 512]),
            "w_q": wq,
            "w_qdec": np.ascontiguousarray(wqdec[:, hs:hs + 512]),
            "w_krope": np.ascontiguousarray(wkrope[:, hr:hr + 256]),
            "w_qrope": np.ascontiguousarray(wqrope[:, hr:hr + 256]),
            "wo_a": np.ascontiguousarray(wo[hs:hs + 512, :]),
            "wo_b": np.ascontiguousarray(wo[512 - hs:1024 - hs, :]),
            "cent_pc": cent_pc,
            "lng_pc": lng_pc,
            "lnb_pc": lnb_pc,
            "b1_pc": b1_pc,
            "b2_pc": b2_pc,
            "w1t": w1t,
            "w2t": w2t,
            "ids_a": np.ascontiguousarray(ids_a),
            "ids_b": np.ascontiguousarray(ids_b),
        })

    if "nc" not in _NC_CACHE:
        _NC_CACHE["nc"] = build_program()
    nc = _NC_CACHE["nc"]
    res = run_bass_kernel_spmd(nc, in_maps, core_ids=list(range(8)))
    out = np.empty((4, T, D), dtype=np.float32)
    for c in range(8):
        seq = c & 3
        hh = c >> 2
        out[seq, hh * OWN:(hh + 1) * OWN] = res.results[c]["out_own"]
    return out
